# version 02 (final)
# baseline (speedup 1.0000x reference)
import contextlib
import numpy as np
import concourse.bass as bass
import concourse.mybir as mybir
from concourse.bass_utils import run_bass_kernel_spmd
from concourse.alu_op_type import AluOpType as OP

F32 = mybir.dt.float32
BF16 = mybir.dt.bfloat16
AF = mybir.ActivationFunctionType
AX = mybir.AxisListType.X

D = 2048
KC = 16
TC = 1024
TT = 512
NCORE = 8
BIG = 1.0e9
BIGI = 16384.0



class Ctx:
    def __init__(self, nc):
        self.nc = nc
        self.es = contextlib.ExitStack()

    def sb(self, name, shape, dtype=F32):
        return self.es.enter_context(self.nc.sbuf_tensor(name, shape, dtype))

    def ps(self, name):
        return self.es.enter_context(self.nc.psum_tensor(name, [128, 512], F32))

    def sem(self, name):
        return self.es.enter_context(self.nc.semaphore(name))

    def close(self):
        self.es.close()


def _mslice(buf, m, width=TC):
    return buf[:, m * width:(m + 1) * width]



def build_l0a():
    nc = bass.Bass("TRN2", target_bir_lowering=False, debug=False)
    c = Ctx(nc)
    x_in = nc.dram_tensor("x_fm", [D, TC], F32, kind="ExternalInput").ap()
    wdtT = nc.dram_tensor("wdtT", [D, D], F32, kind="ExternalInput").ap()
    wbT = nc.dram_tensor("wbT", [D, D], F32, kind="ExternalInput").ap()
    bdt_col = nc.dram_tensor("bdt_col", [128, KC], F32, kind="ExternalInput").ap()
    alog_col = nc.dram_tensor("alog_col", [128, KC], F32, kind="ExternalInput").ap()
    xn_out = nc.dram_tensor("xn", [D, TC], F32, kind="ExternalOutput").ap()
    hloc_out = nc.dram_tensor("hloc", [D, TC], F32, kind="ExternalOutput").ap()
    pcum_out = nc.dram_tensor("pcum", [D, TC], F32, kind="ExternalOutput").ap()
    bnd_out = nc.dram_tensor("bnd", [128, 2 * KC], F32, kind="ExternalOutput").ap()

    xbuf = c.sb("xbuf", [128, KC * TC])
    xnbf = c.sb("xnbf", [128, KC * TC], BF16)
    bdt = c.sb("bdt", [128, KC]); alog = c.sb("alog", [128, KC])
    nacol = c.sb("nacol", [128, KC]); eacol = c.sb("eacol", [128, KC])
    xmbf = c.sb("xmbf", [128, 2 * TC], BF16)
    xsqbf = c.sb("xsqbf", [128, 2 * TC], BF16)
    rows = c.sb("rows", [1, 4 * TC])
    ones1 = c.sb("ones1", [1, 128])
    onesb = c.sb("onesb", [128, 1], BF16)
    meanbc = c.sb("meanbc", [128, TC]); rstdbc = c.sb("rstdbc", [128, TC])
    xnf = c.sb("xnf", [128, 2 * TC])
    wstg = c.sb("wstg", [128, 2 * D])
    wdtb = c.sb("wdtb", [128, 2 * D], BF16)
    wbb = c.sb("wbb", [128, 2 * D], BF16)
    Tdt = c.sb("Tdt", [128, TC]); Tbp = c.sb("Tbp", [128, TC])
    Ts1 = c.sb("Ts1", [128, TC]); Ts2 = c.sb("Ts2", [128, TC])
    Tab = c.sb("Tab", [128, TC])
    hld = c.sb("hld", [128, 2 * TC]); pcd = c.sb("pcd", [128, 2 * TC])
    bnd_sb = c.sb("bnd_sb", [128, 2 * KC])

    pb = [c.ps(f"pb{i}") for i in range(8)]

    s_ld = c.sem("ld"); s_wl = c.sem("wl"); s_cst = c.sem("cst"); s_cst2 = c.sem("cst2")
    s_smm = c.sem("smm"); s_vr = c.sem("vr"); s_sc = c.sem("sc")
    s_vb = c.sem("vb"); s_bmm = c.sem("bmm"); s_vxn = c.sem("vxn")
    s_sxn = c.sem("sxn"); s_od = c.sem("od"); s_mm = c.sem("mm")
    s_ev = c.sem("ev"); s_vz = c.sem("vz"); s_vs = c.sem("vs")
    s_aux = c.sem("aux")

    LD = 4 * 16

    with nc.Block() as block:
        @block.sync
        def _(sync):
            rr_in = x_in.rearrange("(m p) t -> p m t", p=128)
            rr_out = xbuf[:].rearrange("p (m t) -> p m t", t=TC)
            sync.dma_start(out=rr_out[:, 0:8], in_=rr_in[:, 0:8]).then_inc(s_ld, 16)
            sync.dma_start(out=rr_out[:, 8:16], in_=rr_in[:, 8:16]).then_inc(s_ld, 16)
            sync.dma_start(out=bdt[:], in_=bdt_col).then_inc(s_ld, 16)
            sync.dma_start(out=alog[:], in_=alog_col).then_inc(s_ld, 16)
            for i in range(2 * KC):
                m, mat = i // 2, i % 2
                if i >= 2:
                    sync.wait_ge(s_cst, KC + (i - 1))
                w_ap = wdtT if mat == 0 else wbT
                sync.dma_start(
                    out=wstg[:, (i % 2) * D:(i % 2) * D + D].rearrange("p (k i) -> p k i", i=128),
                    in_=w_ap[:, m * 128:(m + 1) * 128].rearrange("(k p) i -> p k i", p=128),
                ).then_inc(s_wl, 16)

        @block.scalar
        def _(scalar):
            scalar.wait_ge(s_ld, LD)
            scalar.activation(eacol[:], alog[:], AF.Exp).then_inc(s_aux, 1)
            for m in range(KC):
                if m >= 2:
                    scalar.wait_ge(s_smm, m - 1)
                scalar.activation(xsqbf[:, (m % 2) * TC:(m % 2) * TC + TC],
                                  _mslice(xbuf, m), AF.Square).then_inc(s_cst2, 1)

        @block.vector
        def _(vector):
            vector.wait_ge(s_ld, LD)
            vector.memset(onesb[:], 1.0)
            vector.memset(ones1[:], 1.0)
            vector.wait_ge(s_aux, 1)
            vector.tensor_scalar_mul(nacol[:], eacol[:], -1.0)
            for m in range(KC):
                if m >= 2:
                    vector.wait_ge(s_smm, m - 1)
                vector.tensor_copy(xmbf[:, (m % 2) * TC:(m % 2) * TC + TC],
                                   _mslice(xbuf, m)).then_inc(s_cst, 1)

        @block.tensor
        def _(tensor):
            for m in range(KC):
                tensor.wait_ge(s_cst, m + 1)
                tensor.wait_ge(s_cst2, m + 1)
                b = (m % 2) * TC
                st = (m == 0); sp = (m == KC - 1)
                tensor.matmul(pb[0][0:1, 0:512], onesb[:], xmbf[:, b:b + 512], start=st, stop=sp)
                tensor.matmul(pb[1][0:1, 0:512], onesb[:], xmbf[:, b + 512:b + 1024], start=st, stop=sp)
                tensor.matmul(pb[2][0:1, 0:512], onesb[:], xsqbf[:, b:b + 512], start=st, stop=sp)
                tensor.matmul(pb[3][0:1, 0:512], onesb[:], xsqbf[:, b + 512:b + 1024],
                              start=st, stop=sp).then_inc(s_smm, 1)

        @block.vector
        def _(vector):
            vector.wait_ge(s_smm, KC)
            inv = 1.0 / float(D)
            vector.tensor_scalar(rows[0:1, 0:512], pb[0][0:1, 0:512], inv, None, OP.mult)
            vector.tensor_scalar(rows[0:1, 512:1024], pb[1][0:1, 0:512], inv, None, OP.mult)
            vector.tensor_scalar(rows[0:1, 1024:1536], pb[2][0:1, 0:512], inv, None, OP.mult)
            vector.tensor_scalar(rows[0:1, 1536:2048], pb[3][0:1, 0:512], inv, None, OP.mult)
            vector.tensor_mul(rows[0:1, 2048:3072], rows[0:1, 0:1024], rows[0:1, 0:1024])
            vector.tensor_sub(rows[0:1, 2048:3072], rows[0:1, 1024:2048], rows[0:1, 2048:3072])
            vector.tensor_scalar_add(rows[0:1, 2048:3072], rows[0:1, 2048:3072], 1.0e-5).then_inc(s_vr, 1)

        @block.scalar
        def _(scalar):
            scalar.wait_ge(s_vr, 1)
            scalar.activation(rows[0:1, 1024:2048], rows[0:1, 2048:3072], AF.Sqrt).then_inc(s_sc, 1)

        @block.vector
        def _(vector):
            vector.wait_ge(s_sc, 1)
            vector.reciprocal(rows[0:1, 3072:4096], rows[0:1, 1024:2048]).then_inc(s_vb, 1)

        @block.tensor
        def _(tensor):
            tensor.wait_ge(s_vb, 1)
            tensor.matmul(pb[4][:, 0:512], ones1[0:1, :], rows[0:1, 0:512], start=True, stop=True)
            tensor.matmul(pb[5][:, 0:512], ones1[0:1, :], rows[0:1, 512:1024], start=True, stop=True)
            tensor.matmul(pb[6][:, 0:512], ones1[0:1, :], rows[0:1, 3072:3584], start=True, stop=True)
            tensor.matmul(pb[7][:, 0:512], ones1[0:1, :], rows[0:1, 3584:4096],
                          start=True, stop=True).then_inc(s_bmm, 1)

        @block.vector
        def _(vector):
            vector.wait_ge(s_bmm, 1)
            vector.tensor_copy(meanbc[:, 0:512], pb[4][:, 0:512])
            vector.tensor_copy(meanbc[:, 512:1024], pb[5][:, 0:512])
            vector.tensor_copy(rstdbc[:, 0:512], pb[6][:, 0:512])
            vector.tensor_copy(rstdbc[:, 512:1024], pb[7][:, 0:512])
            for m in range(KC):
                if m >= 2:
                    vector.wait_ge(s_od, 16 * (m - 1))
                    vector.wait_ge(s_sxn, m - 1)
                b = (m % 2) * TC
                vector.tensor_sub(Ts1[:], _mslice(xbuf, m), meanbc[:])
                vector.tensor_mul(xnf[:, b:b + TC], Ts1[:], rstdbc[:]).then_inc(s_vxn, 1)

        @block.gpsimd
        def _(gpsimd):
            for m in range(KC):
                gpsimd.wait_ge(s_vxn, m + 1)
                gpsimd.dma_start(out=xn_out[m * 128:(m + 1) * 128, :],
                                 in_=xnf[:, (m % 2) * TC:(m % 2) * TC + TC]).then_inc(s_od, 16)

        @block.scalar
        def _(scalar):
            for m in range(KC):
                scalar.wait_ge(s_vxn, m + 1)
                scalar.copy(_mslice(xnbf, m), xnf[:, (m % 2) * TC:(m % 2) * TC + TC]).then_inc(s_sxn, 1)


        @block.tensor
        def _(tensor):
            tensor.wait_ge(s_sxn, KC)
            for m in range(KC):
                tensor.wait_ge(s_cst, KC + 2 * (m + 1))
                if m >= 2:
                    tensor.wait_ge(s_ev, 2 * (m - 1))
                b = (m % 2) * D
                pk = [pb[0], pb[1], pb[2], pb[3]] if m % 2 == 0 else [pb[4], pb[5], pb[6], pb[7]]
                for k in range(KC):
                    st = (k == 0); sp = (k == KC - 1)
                    lhs = wdtb[:, b + k * 128:b + (k + 1) * 128]
                    tensor.matmul(pk[0][:, 0:512], lhs, xnbf[:, k * TC:k * TC + 512], start=st, stop=sp)
                    tensor.matmul(pk[1][:, 0:512], lhs, xnbf[:, k * TC + 512:k * TC + 1024], start=st, stop=sp)
                for k in range(KC):
                    st = (k == 0); sp = (k == KC - 1)
                    lhs = wbb[:, b + k * 128:b + (k + 1) * 128]
                    tensor.matmul(pk[2][:, 0:512], lhs, xnbf[:, k * TC:k * TC + 512], start=st, stop=sp)
                    ins = tensor.matmul(pk[3][:, 0:512], lhs, xnbf[:, k * TC + 512:k * TC + 1024], start=st, stop=sp)
                    if sp:
                        ins.then_inc(s_mm, 1)

        @block.vector
        def _(vector):
            def cast_weights(m):
                for mat in range(2):
                    i = 2 * m + mat
                    vector.wait_ge(s_wl, 16 * (i + 1))
                    if m >= 2 and mat == 0:
                        vector.wait_ge(s_mm, m - 1)
                    dst = wdtb if mat == 0 else wbb
                    vector.tensor_copy(dst[:, (m % 2) * D:(m % 2) * D + D],
                                       wstg[:, (i % 2) * D:(i % 2) * D + D]).then_inc(s_cst, 1)

            def evac_math(m):
                vector.wait_ge(s_mm, m + 1)
                if m >= 1:
                    vector.wait_ge(s_sc, 1 + 2 * (m - 1) + 1)
                pk = [pb[0], pb[1], pb[2], pb[3]] if m % 2 == 0 else [pb[4], pb[5], pb[6], pb[7]]
                vector.tensor_scalar(Tdt[:, 0:512], pk[0][:, 0:512], bdt[:, m:m + 1], None, OP.add)
                vector.tensor_scalar(Tdt[:, 512:1024], pk[1][:, 0:512], bdt[:, m:m + 1], None, OP.add).then_inc(s_ev, 1)
                vector.tensor_copy(Tbp[:, 0:512], pk[2][:, 0:512])
                vector.tensor_copy(Tbp[:, 512:1024], pk[3][:, 0:512]).then_inc(s_ev, 1)
                vector.wait_ge(s_sc, 1 + 2 * m + 1)
                vector.tensor_scalar_max(Ts2[:], Tdt[:], 0.0)
                vector.tensor_add(Tdt[:], Ts2[:], Ts1[:])
                vector.tensor_scalar(Ts2[:], Tdt[:], nacol[:, m:m + 1], None, OP.mult).then_inc(s_vz, 1)
                vector.wait_ge(s_sc, 1 + 2 * m + 2)
                vector.tensor_mul(Ts1[:], Tdt[:], Tbp[:])
                vector.tensor_mul(Ts2[:], Ts1[:], _mslice(xnbf, m))
                bs = (m % 2) * TC
                if m >= 2:
                    vector.wait_ge(s_od, 16 * (KC + 2 * (m - 1)))
                vector.tensor_tensor_scan(hld[:, bs:bs + TC], Tab[:], Ts2[:], 0.0, OP.mult, OP.add)
                vector.tensor_tensor_scan(pcd[:, bs:bs + TC], Tab[:], Tab[:], 1.0, OP.mult, OP.bypass)
                vector.tensor_copy(bnd_sb[:, 2 * m:2 * m + 1], hld[:, bs + TC - 1:bs + TC])
                vector.tensor_copy(bnd_sb[:, 2 * m + 1:2 * m + 2], pcd[:, bs + TC - 1:bs + TC]).then_inc(s_vs, 1)

            for m in range(KC):
                cast_weights(m)
                if m >= 1:
                    evac_math(m - 1)
            evac_math(KC - 1)

        @block.scalar
        def _(scalar):
            for m in range(KC):
                scalar.wait_ge(s_ev, 2 * m + 1)
                scalar.activation(Ts1[:], Tdt[:], AF.Abs)
                scalar.activation(Ts1[:], Ts1[:], AF.Exp, scale=-1.0)
                scalar.activation(Ts1[:], Ts1[:], AF.Ln, bias=1.0).then_inc(s_sc, 1)
                scalar.wait_ge(s_vz, m + 1)
                scalar.activation(Tab[:], Ts2[:], AF.Exp).then_inc(s_sc, 1)

        @block.gpsimd
        def _(gpsimd):
            for m in range(KC):
                gpsimd.wait_ge(s_vs, m + 1)
                bs = (m % 2) * TC
                gpsimd.dma_start(out=hloc_out[m * 128:(m + 1) * 128, :], in_=hld[:, bs:bs + TC]).then_inc(s_od, 16)
                gpsimd.dma_start(out=pcum_out[m * 128:(m + 1) * 128, :], in_=pcd[:, bs:bs + TC]).then_inc(s_od, 16)
            gpsimd.wait_ge(s_vs, KC)
            gpsimd.dma_start(out=bnd_out, in_=bnd_sb[:]).then_inc(s_od, 16)
            gpsimd.wait_ge(s_od, 16 * (KC + 2 * KC + 1))

    c.close()
    return nc



def build_l0b(debug=False):
    nc = bass.Bass("TRN2", target_bir_lowering=False, debug=False)
    c = Ctx(nc)
    x_in = nc.dram_tensor("x_fm", [D, TC], F32, kind="ExternalInput").ap()
    xn_in = nc.dram_tensor("xn", [D, TC], F32, kind="ExternalInput").ap()
    hloc_in = nc.dram_tensor("hloc", [D, TC], F32, kind="ExternalInput").ap()
    pcum_in = nc.dram_tensor("pcum", [D, TC], F32, kind="ExternalInput").ap()
    bndb = nc.dram_tensor("bndb", [128, 128], F32, kind="ExternalInput").ap()
    cmask = nc.dram_tensor("cmask", [128, 4], F32, kind="ExternalInput").ap()
    wcT = nc.dram_tensor("wcT", [D, D], F32, kind="ExternalInput").ap()
    dskip_col = nc.dram_tensor("dskip_col", [128, KC], F32, kind="ExternalInput").ap()
    rwT = nc.dram_tensor("rwT", [D, 16], F32, kind="ExternalInput").ap()
    ebias_bc = nc.dram_tensor("ebias_bc", [128, 16], F32, kind="ExternalInput").ap()
    iota_in = nc.dram_tensor("iota_bc", [128, 16], F32, kind="ExternalInput").ap()
    ident_in = nc.dram_tensor("ident", [128, 128], F32, kind="ExternalInput").ap()
    eye16_in = nc.dram_tensor("eye16", [16, 2048], BF16, kind="ExternalInput").ap()
    wup_in = nc.dram_tensor("w_up", [16, D, 1024], F32, kind="ExternalInput").ap()
    wdn_in = nc.dram_tensor("w_down", [16, 1024, D], F32, kind="ExternalInput").ap()
    shup_in = nc.dram_tensor("shared_up", [D, 4096], F32, kind="ExternalInput").ap()
    shdn_in = nc.dram_tensor("shared_down", [4096, D], F32, kind="ExternalInput").ap()
    x1_out = nc.dram_tensor("x1", [D, TC], F32, kind="ExternalOutput").ap()
    if debug:
        dbg_comb = nc.dram_tensor("dbg_comb", [128, 128], F32, kind="ExternalOutput").ap()
        dbg_combT = nc.dram_tensor("dbg_combT", [16, TC], F32, kind="ExternalOutput").ap()
        dbg_bcf = nc.dram_tensor("dbg_bcf", [128, TC], F32, kind="ExternalOutput").ap()
        dbg_abf = nc.dram_tensor("dbg_abf", [128, 8 * TC], F32, kind="ExternalOutput").ap()
        dbg_scoreS = nc.dram_tensor("dbg_scoreS", [128, 128], F32, kind="ExternalOutput").ap()

    xbuf = c.sb("xbuf", [128, KC * TC])
    xnbf = c.sb("xnbf", [128, KC * TC], BF16)
    A1 = c.sb("A1", [128, KC * TC], BF16)
    xnstg = c.sb("xnstg", [128, TC])
    hpstg = c.sb("hpstg", [128, 2 * TC])
    wstg = c.sb("wstg", [128, 2 * 2048])
    wcb = c.sb("wcb", [128, 2 * 2048], BF16)
    rwstg = c.sb("rwstg", [128, 16 * KC]); rwb = c.sb("rwb", [128, 16 * KC], BF16)
    bnd_sb = c.sb("bnd_sb", [128, 128]); cm_sb = c.sb("cm_sb", [128, 4])
    carry = c.sb("carry", [128, KC]); dskip = c.sb("dskip", [128, KC])
    ebias = c.sb("ebias", [128, 16]); iota = c.sb("iota", [128, 16])
    iMB = c.sb("iMB", [128, 16])
    ident = c.sb("ident_sb", [128, 128]); eye16 = c.sb("eye16_sb", [16, 2048], BF16)
    scoreS = c.sb("scoreS", [128, 8 * 16])
    comb = c.sb("comb", [128, 8 * 16])
    combT = c.sb("combT", [16, TC], BF16)
    rS = c.sb("rS", [128, 16]); rE = c.sb("rE", [128, 16])
    r1 = c.sb("r1", [128, 16]); r2 = c.sb("r2", [128, 16]); r3 = c.sb("r3", [128, 16])
    r4 = c.sb("r4", [128, 8]); gsum = c.sb("gsum", [128, 4])
    rc1 = c.sb("rc1", [128, 1]); rc2 = c.sb("rc2", [128, 1])
    abf = c.sb("abf", [128, 8 * TC], BF16)
    bcf = c.sb("bcf", [128, TC])
    T1 = c.sb("T1", [128, 2 * TC]); T2 = c.sb("T2", [128, 2 * TC], BF16)

    pb = [c.ps(f"pb{i}") for i in range(8)]

    s_ld = c.sem("ld"); s_w1 = c.sem("w1"); s_sxn = c.sem("sxn")
    s_w2 = c.sem("w2"); s_c2 = c.sem("c2")
    s_w3 = c.sem("w3"); s_c3 = c.sem("c3")
    s_crw = c.sem("crw"); s_sig = c.sem("sig")
    s_mm = c.sem("mm"); s_ev = c.sem("ev")
    s_tr = c.sem("tr"); s_vct = c.sem("vct"); s_vcm = c.sem("vcm")
    s_bc = c.sem("bc"); s_bcv = c.sem("bcv")
    s_sq = c.sem("sq"); s_vmul = c.sem("vmul"); s_od = c.sem("od")
    s_dr = c.sem("dr")

    NG_Q2, NG_RT = KC, 8
    BASE_MOE = NG_Q2 + NG_RT
    g_up = lambda e, ich: BASE_MOE + e * 24 + ich
    g_dn = lambda e, dch: BASE_MOE + e * 24 + 8 + dch
    BASE_SH = BASE_MOE + 16 * 24
    g_upsh = lambda qh, ich: BASE_SH + qh * 24 + ich
    g_dnsh = lambda qh, dch: BASE_SH + qh * 24 + 8 + dch
    G_TOTAL = BASE_SH + 4 * 24

    grp_witem = []
    wi = 0
    for m in range(KC):
        grp_witem.append(wi); wi += 1
    for tc in range(NG_RT):
        grp_witem.append(None)
    for e in range(16):
        for r in range(24):
            grp_witem.append(wi); wi += 1
    for qh in range(4):
        for r in range(24):
            grp_witem.append(wi); wi += 1
    NW = wi
    witem_grp = {}
    for gg, ww in enumerate(grp_witem):
        if ww is not None:
            witem_grp[ww] = gg

    def witem_src(i):
        j = i
        if j < KC:
            m = j
            return wcT[:, m * 128:(m + 1) * 128].rearrange("(k p) i -> p k i", p=128), 2048
        j -= KC
        if j < 16 * 24:
            e, r = j // 24, j % 24
            if r < 8:
                ich = r
                return wup_in[e, :, ich * 128:(ich + 1) * 128].rearrange("(k p) i -> p k i", p=128), 2048
            dch = r - 8
            return wdn_in[e, :, dch * 128:(dch + 1) * 128].rearrange("(k p) i -> p k i", p=128), 1024
        j -= 16 * 24
        qh, r = j // 24, j % 24
        if r < 8:
            col = qh * 1024 + r * 128
            return shup_in[:, col:col + 128].rearrange("(k p) i -> p k i", p=128), 2048
        dch = r - 8
        return shdn_in[qh * 1024:(qh + 1) * 1024, dch * 128:(dch + 1) * 128].rearrange("(k p) i -> p k i", p=128), 1024

    LD = 16 * 10

    with nc.Block() as block:
        @block.sync
        def _(sync):
            rr_in = x_in.rearrange("(m p) t -> p m t", p=128)
            rr_out = xbuf[:].rearrange("p (m t) -> p m t", t=TC)
            sync.dma_start(out=rr_out[:, 0:8], in_=rr_in[:, 0:8]).then_inc(s_ld, 16)
            sync.dma_start(out=rr_out[:, 8:16], in_=rr_in[:, 8:16]).then_inc(s_ld, 16)
            sync.dma_start(out=bnd_sb[:], in_=bndb).then_inc(s_ld, 16)
            sync.dma_start(out=cm_sb[:], in_=cmask).then_inc(s_ld, 16)
            sync.dma_start(out=dskip[:], in_=dskip_col).then_inc(s_ld, 16)
            sync.dma_start(out=ebias[:], in_=ebias_bc).then_inc(s_ld, 16)
            sync.dma_start(out=iota[:], in_=iota_in).then_inc(s_ld, 16)
            sync.dma_start(out=ident[:], in_=ident_in).then_inc(s_ld, 16)
            sync.dma_start(out=eye16[:], in_=eye16_in).then_inc(s_ld, 16)
            sync.dma_start(out=rwstg[:].rearrange("p (k e) -> p k e", e=16),
                           in_=rwT.rearrange("(k p) e -> p k e", p=128)).then_inc(s_ld, 16)
            for i in range(KC):
                if i >= 1:
                    sync.wait_ge(s_sxn, i)
                sync.dma_start(out=xnstg[:, 0:TC],
                               in_=xn_in[i * 128:(i + 1) * 128, :]).then_inc(s_w1, 16)
            for i in range(2 * KC):
                if i >= 2:
                    sync.wait_ge(s_c2, i if i % 2 == 0 else i - 2)
                src_ap = hloc_in if i % 2 == 0 else pcum_in
                k = i // 2
                sync.dma_start(out=hpstg[:, (i % 2) * TC:(i % 2) * TC + TC],
                               in_=src_ap[k * 128:(k + 1) * 128, :]).then_inc(s_w2, 16)
            for i in range(NW):
                if i >= 2:
                    sync.wait_ge(s_c3, i - 1)
                src_ap, ncol = witem_src(i)
                sync.dma_start(
                    out=wstg[:, (i % 2) * 2048:(i % 2) * 2048 + ncol].rearrange("p (k i) -> p k i", i=128),
                    in_=src_ap).then_inc(s_w3, 16)

        @block.gpsimd
        def _(gpsimd):
            gpsimd.wait_ge(s_ld, LD)
            gpsimd.tensor_copy(rwb[:], rwstg[:]).then_inc(s_crw, 1)
            for i in range(NW):
                gpsimd.wait_ge(s_w3, 16 * (i + 1))
                if i >= 2:
                    gpsimd.wait_ge(s_mm, witem_grp[i - 2] + 1)
                n = witem_src(i)[1]
                gpsimd.tensor_copy(wcb[:, (i % 2) * 2048:(i % 2) * 2048 + n],
                                   wstg[:, (i % 2) * 2048:(i % 2) * 2048 + n]).then_inc(s_c3, 1)
            gpsimd.wait_ge(s_ev, G_TOTAL)
            rr_o = x1_out.rearrange("(m p) t -> p m t", p=128)
            rr_i = xbuf[:].rearrange("p (m t) -> p m t", t=TC)
            gpsimd.dma_start(out=rr_o[:, 0:8], in_=rr_i[:, 0:8]).then_inc(s_od, 16)
            gpsimd.dma_start(out=rr_o[:, 8:16], in_=rr_i[:, 8:16]).then_inc(s_od, 16)
            nod = 32
            if debug:
                gpsimd.dma_start(out=dbg_comb, in_=comb[:]).then_inc(s_od, 16)
                gpsimd.dma_start(out=dbg_scoreS, in_=scoreS[:]).then_inc(s_od, 16)
                gpsimd.tensor_copy(T1[:, 0:TC], combT[0:16, :].rearrange("e t -> e t") if False else T1[:, 0:TC])
                nod += 32
                gpsimd.tensor_copy(T1[0:16, 0:TC], combT[0:16, :])
                gpsimd.dma_start(out=dbg_combT, in_=T1[0:16, 0:TC]).then_inc(s_od, 16)
                gpsimd.dma_start(out=dbg_bcf, in_=bcf[:]).then_inc(s_od, 16)
                nod += 32
                for ich in range(8):
                    gpsimd.tensor_copy(T1[:, 0:TC], abf[:, ich * TC:(ich + 1) * TC])
                    gpsimd.dma_start(out=dbg_abf[:, ich * TC:(ich + 1) * TC],
                                     in_=T1[:, 0:TC]).then_inc(s_od, 16)
                    nod += 16
            gpsimd.wait_ge(s_od, nod)

        @block.scalar
        def _(scalar):
            for i in range(KC):
                scalar.wait_ge(s_w1, 16 * (i + 1))
                scalar.copy(_mslice(xnbf, i), xnstg[:, 0:TC]).then_inc(s_sxn, 1)
            for tc in range(8):
                scalar.wait_ge(s_ev, NG_Q2 + tc + 1)
                scalar.activation(scoreS[:, tc * 16:tc * 16 + 16],
                                  scoreS[:, tc * 16:tc * 16 + 16], AF.Sigmoid).then_inc(s_sig, 1)
            for e in range(16):
                for ich in range(8):
                    uidx = e * 8 + ich
                    bh = uidx % 2
                    scalar.wait_ge(s_ev, g_up(e, ich) + 1)
                    if uidx >= 2:
                        scalar.wait_ge(s_vmul, uidx - 1)
                    scalar.activation(T2[:, bh * TC:bh * TC + TC],
                                      T1[:, bh * TC:bh * TC + TC], AF.Square).then_inc(s_sq, 1)
            for qh in range(4):
                for ich in range(8):
                    shidx = qh * 8 + ich
                    bh = shidx % 2
                    scalar.wait_ge(s_ev, g_upsh(qh, ich) + 1)
                    if qh >= 1 and ich == 0:
                        scalar.wait_ge(s_mm, g_dnsh(qh - 1, KC - 1) + 1)
                    scalar.activation(_mslice(abf, ich),
                                      T1[:, bh * TC:bh * TC + TC], AF.Square).then_inc(s_sq, 1)

        @block.tensor
        def _(tensor):
            tensor.wait_ge(s_sxn, KC)

            def pre_group(g):
                if grp_witem[g] is not None:
                    tensor.wait_ge(s_c3, grp_witem[g] + 1)
                if g >= 2:
                    tensor.wait_ge(s_ev, g - 1)
                return [pb[0], pb[1]] if g % 2 == 0 else [pb[2], pb[3]]

            def up_group(g, rhs_buf):
                pk = pre_group(g)
                wb = (grp_witem[g] % 2) * 2048
                for k in range(KC):
                    st = (k == 0); sp = (k == KC - 1)
                    lhs = wcb[:, wb + k * 128:wb + (k + 1) * 128]
                    tensor.matmul(pk[0][:, 0:512], lhs, rhs_buf[:, k * TC:k * TC + 512], start=st, stop=sp)
                    ins = tensor.matmul(pk[1][:, 0:512], lhs, rhs_buf[:, k * TC + 512:k * TC + 1024], start=st, stop=sp)
                    if sp:
                        ins.then_inc(s_mm, 1)

            def dn_group(g):
                pk = pre_group(g)
                wb = (grp_witem[g] % 2) * 2048
                for ich in range(8):
                    st = (ich == 0); sp = (ich == 7)
                    lhs = wcb[:, wb + ich * 128:wb + (ich + 1) * 128]
                    tensor.matmul(pk[0][:, 0:512], lhs, abf[:, ich * TC:ich * TC + 512], start=st, stop=sp)
                    ins = tensor.matmul(pk[1][:, 0:512], lhs, abf[:, ich * TC + 512:ich * TC + 1024], start=st, stop=sp)
                    if sp:
                        ins.then_inc(s_mm, 1)

            g = 0
            for m in range(KC):
                if m == 0:
                    tensor.wait_ge(s_c2, 2 * KC)
                up_group(g, A1); g += 1
            tensor.wait_ge(s_crw, 1)
            for tc in range(8):
                pk = pre_group(g)
                for k in range(KC):
                    st = (k == 0); sp = (k == KC - 1)
                    ins = tensor.matmul(pk[0][:, 0:16],
                                        xnbf[:, k * TC + tc * 128:k * TC + (tc + 1) * 128],
                                        rwb[:, k * 16:(k + 1) * 16], start=st, stop=sp)
                    if sp:
                        ins.then_inc(s_mm, 1)
                g += 1
            for tc in range(8):
                tensor.wait_ge(s_vcm, tc + 1)
                pdst = pb[4] if tc < 4 else pb[5]
                tensor.transpose(pdst[0:16, (tc % 4) * 128:(tc % 4) * 128 + 128],
                                 comb[:, tc * 16:tc * 16 + 16], ident[:]).then_inc(s_tr, 1)
            for e in range(16):
                tensor.wait_ge(s_vct, 2)
                if e >= 1:
                    tensor.wait_ge(s_bcv, e)
                tensor.matmul(pb[6][:, 0:512], eye16[:, e * 128:(e + 1) * 128], combT[0:16, 0:512],
                              start=True, stop=True)
                tensor.matmul(pb[7][:, 0:512], eye16[:, e * 128:(e + 1) * 128], combT[0:16, 512:1024],
                              start=True, stop=True).then_inc(s_bc, 1)
                for ich in range(8):
                    up_group(g, xnbf); g += 1
                tensor.wait_ge(s_vmul, 8 * (e + 1))
                for dch in range(KC):
                    dn_group(g); g += 1
            for qh in range(4):
                for ich in range(8):
                    up_group(g, xnbf); g += 1
                tensor.wait_ge(s_sq, 128 + 8 * (qh + 1))
                for dch in range(KC):
                    dn_group(g); g += 1
            assert g == G_TOTAL

        @block.vector
        def _(vector):
            ndr = [0]
            def step(ins):
                ndr[0] += 1
                ins.then_inc(s_dr, 1)
                vector.wait_ge(s_dr, ndr[0])
            vector.wait_ge(s_ld, LD)
            vector.tensor_scalar(iMB[:], iota[:], -BIGI, None, OP.add)
            vector.memset(carry[:], 0.0)
            for m in range(KC):
                for j in range(4):
                    E = bnd_sb[:, j * 32 + 2 * m:j * 32 + 2 * m + 1]
                    P = bnd_sb[:, j * 32 + 2 * m + 1:j * 32 + 2 * m + 2]
                    step(vector.tensor_mul(rc1[:], P, carry[:, m:m + 1]))
                    step(vector.tensor_add(rc1[:], rc1[:], E))
                    step(vector.tensor_sub(rc2[:], rc1[:], carry[:, m:m + 1]))
                    step(vector.scalar_tensor_tensor(carry[:, m:m + 1], rc2[:], cm_sb[:, j:j + 1],
                                                carry[:, m:m + 1], OP.mult, OP.add))
            for k in range(KC):
                vector.wait_ge(s_w2, 16 * (2 * k + 2))
                vector.tensor_scalar(T1[:, 0:TC], hpstg[:, TC:2 * TC], carry[:, k:k + 1],
                                     None, OP.mult).then_inc(s_c2, 1)
                vector.tensor_add(_mslice(A1, k), T1[:, 0:TC], hpstg[:, 0:TC]).then_inc(s_c2, 1)
            for m in range(KC):
                g = m
                vector.wait_ge(s_mm, g + 1)
                pk = [pb[0], pb[1]] if g % 2 == 0 else [pb[2], pb[3]]
                for tt in range(2):
                    sl = slice(m * TC + tt * 512, m * TC + tt * 512 + 512)
                    ins = vector.tensor_add(T1[:, 0:512], pk[tt][:, 0:512], xbuf[:, sl])
                    if tt == 1:
                        ins.then_inc(s_ev, 1)
                    vector.scalar_tensor_tensor(xbuf[:, sl], xnbf[:, sl], dskip[:, m:m + 1],
                                                T1[:, 0:512], OP.mult, OP.add)
            for tc in range(8):
                g = NG_Q2 + tc
                vector.wait_ge(s_mm, g + 1)
                pk = [pb[0], pb[1]] if g % 2 == 0 else [pb[2], pb[3]]
                vector.tensor_copy(scoreS[:, tc * 16:tc * 16 + 16], pk[0][:, 0:16]).then_inc(s_ev, 1)
            for tc in range(8):
                vector.wait_ge(s_sig, tc + 1)
                sc = scoreS[:, tc * 16:tc * 16 + 16]
                step(vector.tensor_add(rS[:], sc, ebias[:]))
                for grp in range(4):
                    seg = rS[:, grp * 4:grp * 4 + 4]
                    step(vector.reduce_max(rc1[:], seg, axis=AX))
                    step(vector.tensor_scalar(r4[:, 0:4], seg, rc1[:, 0:1], None, OP.is_equal))
                    step(vector.scalar_tensor_tensor(r4[:, 4:8], r4[:, 0:4], -BIG, seg, OP.mult, OP.add))
                    step(vector.reduce_max(rc2[:], r4[:, 4:8], axis=AX))
                    step(vector.tensor_add(gsum[:, grp:grp + 1], rc1[:], rc2[:]))
                step(vector.reduce_max(rc1[:], gsum[:], axis=AX))
                step(vector.tensor_scalar(r4[:, 0:4], gsum[:], rc1[:, 0:1], None, OP.is_equal))
                step(vector.scalar_tensor_tensor(r4[:, 4:8], r4[:, 0:4], -BIG, gsum[:], OP.mult, OP.add))
                step(vector.reduce_max(rc2[:], r4[:, 4:8], axis=AX))
                step(vector.tensor_scalar(r4[:, 0:4], gsum[:], rc2[:, 0:1], None, OP.is_ge))
                rEv = rE[:].rearrange("p (g j) -> p g j", j=4)
                r4v = r4[:, 0:4].rearrange("p (g o) -> p g o", o=1)
                for j in range(4):
                    step(vector.tensor_copy(rEv[:, :, j:j + 1], r4v))
                step(vector.tensor_mul(r2[:], rS[:], rE[:]))
                step(vector.memset(r3[:], 0.0))
                for it in range(2):
                    step(vector.reduce_max(rc1[:], r2[:], axis=AX))
                    step(vector.tensor_scalar(r1[:], r2[:], rc1[:, 0:1], None, OP.is_equal))
                    step(vector.tensor_mul(r1[:], r1[:], iMB[:]))
                    step(vector.tensor_scalar(r1[:], r1[:], BIGI, None, OP.add))
                    step(vector.tensor_reduce(rc2[:], r1[:], axis=AX, op=OP.min))
                    step(vector.tensor_scalar(r1[:], iota[:], rc2[:, 0:1], None, OP.is_equal))
                    step(vector.tensor_add(r3[:], r3[:], r1[:]))
                    step(vector.scalar_tensor_tensor(r2[:], r1[:], -BIG, r2[:], OP.mult, OP.add))
                cw = comb[:, tc * 16:tc * 16 + 16]
                step(vector.tensor_mul(cw, sc, r3[:]))
                step(vector.reduce_sum(rc1[:], cw, axis=AX))
                step(vector.tensor_scalar(rc1[:], rc1[:], 1.0e-20, None, OP.add))
                step(vector.reciprocal(rc2[:], rc1[:]))
                vector.tensor_scalar(cw, cw, rc2[:, 0:1], None, OP.mult).then_inc(s_vcm, 1)
            vector.wait_ge(s_tr, 4)
            vector.tensor_copy(combT[0:16, 0:512], pb[4][0:16, 0:512]).then_inc(s_vct, 1)
            vector.wait_ge(s_tr, 8)
            vector.tensor_copy(combT[0:16, 512:1024], pb[5][0:16, 0:512]).then_inc(s_vct, 1)
            for e in range(16):
                vector.wait_ge(s_bc, e + 1)
                vector.tensor_copy(bcf[:, 0:512], pb[6][:, 0:512])
                vector.tensor_copy(bcf[:, 512:1024], pb[7][:, 0:512]).then_inc(s_bcv, 1)
                for ich in range(8):
                    uidx = e * 8 + ich
                    bh = uidx % 2
                    g = g_up(e, ich)
                    vector.wait_ge(s_mm, g + 1)
                    pk = [pb[0], pb[1]] if g % 2 == 0 else [pb[2], pb[3]]
                    vector.tensor_scalar_max(T1[:, bh * TC:bh * TC + 512], pk[0][:, 0:512], 0.0)
                    vector.tensor_scalar_max(T1[:, bh * TC + 512:bh * TC + 1024],
                                             pk[1][:, 0:512], 0.0).then_inc(s_ev, 1)
                    vector.wait_ge(s_sq, uidx + 1)
                    if ich == 0 and e >= 1:
                        vector.wait_ge(s_mm, g_dn(e - 1, KC - 1) + 1)
                    vector.tensor_mul(_mslice(abf, ich), T2[:, bh * TC:bh * TC + TC],
                                      bcf[:]).then_inc(s_vmul, 1)
                for dch in range(KC):
                    g = g_dn(e, dch)
                    vector.wait_ge(s_mm, g + 1)
                    pk = [pb[0], pb[1]] if g % 2 == 0 else [pb[2], pb[3]]
                    for tt in range(2):
                        sl = slice(dch * TC + tt * 512, dch * TC + tt * 512 + 512)
                        ins = vector.tensor_add(xbuf[:, sl], xbuf[:, sl], pk[tt][:, 0:512])
                        if tt == 1:
                            ins.then_inc(s_ev, 1)
            for qh in range(4):
                for ich in range(8):
                    shidx = qh * 8 + ich
                    bh = shidx % 2
                    g = g_upsh(qh, ich)
                    vector.wait_ge(s_mm, g + 1)
                    if shidx >= 2:
                        vector.wait_ge(s_sq, 128 + shidx - 1)
                    pk = [pb[0], pb[1]] if g % 2 == 0 else [pb[2], pb[3]]
                    vector.tensor_scalar_max(T1[:, bh * TC:bh * TC + 512], pk[0][:, 0:512], 0.0)
                    vector.tensor_scalar_max(T1[:, bh * TC + 512:bh * TC + 1024],
                                             pk[1][:, 0:512], 0.0).then_inc(s_ev, 1)
                for dch in range(KC):
                    g = g_dnsh(qh, dch)
                    vector.wait_ge(s_mm, g + 1)
                    pk = [pb[0], pb[1]] if g % 2 == 0 else [pb[2], pb[3]]
                    for tt in range(2):
                        sl = slice(dch * TC + tt * 512, dch * TC + tt * 512 + 512)
                        ins = vector.tensor_add(xbuf[:, sl], xbuf[:, sl], pk[tt][:, 0:512])
                        if tt == 1:
                            ins.then_inc(s_ev, 1)

    c.close()
    return nc


def build_l1a():
    nc = bass.Bass("TRN2", target_bir_lowering=False, debug=False)
    c = Ctx(nc)
    x_in = nc.dram_tensor("x_fm", [D, TC], F32, kind="ExternalInput").ap()
    gwT = nc.dram_tensor("gwT", [D, D], F32, kind="ExternalInput").ap()
    gb_col = nc.dram_tensor("gb_col", [128, KC], F32, kind="ExternalInput").ap()
    a_col = nc.dram_tensor("a_col", [128, KC], F32, kind="ExternalInput").ap()
    b_col = nc.dram_tensor("b_col", [128, KC], F32, kind="ExternalInput").ap()
    xn_out = nc.dram_tensor("xn1", [D, TC], F32, kind="ExternalOutput").ap()
    ug_out = nc.dram_tensor("ug", [D, TC], F32, kind="ExternalOutput").ap()
    sloc_out = nc.dram_tensor("sloc", [D, TC], F32, kind="ExternalOutput").ap()
    spcum_out = nc.dram_tensor("spcum", [D, TC], F32, kind="ExternalOutput").ap()
    bnd_out = nc.dram_tensor("bnd1", [128, 2 * KC], F32, kind="ExternalOutput").ap()
    upart_out = nc.dram_tensor("upart", [128, KC], F32, kind="ExternalOutput").ap()

    xbuf = c.sb("xbuf", [128, KC * TC])
    xnbf = c.sb("xnbf", [128, KC * TC], BF16)
    gb = c.sb("gb", [128, KC]); acol = c.sb("acol", [128, KC])
    bcol = c.sb("bcol", [128, KC]); aacol = c.sb("aacol", [128, KC])
    xmbf = c.sb("xmbf", [128, 2 * TC], BF16)
    xsqbf = c.sb("xsqbf", [128, 2 * TC], BF16)
    rows = c.sb("rows", [1, 4 * TC])
    ones1 = c.sb("ones1", [1, 128])
    onesb = c.sb("onesb", [128, 1], BF16)
    ones1k = c.sb("ones1k", [128, TC])
    meanbc = c.sb("meanbc", [128, TC]); rstdbc = c.sb("rstdbc", [128, TC])
    xnf = c.sb("xnf", [128, 2 * TC])
    wstg = c.sb("wstg", [128, 2 * D])
    wgb = c.sb("wgb", [128, 2 * D], BF16)
    Tg = c.sb("Tg", [128, TC]); Tg2 = c.sb("Tg2", [128, TC])
    Tug = c.sb("Tug", [128, 2 * TC])
    Tbu = c.sb("Tbu", [128, TC]); Taab = c.sb("Taab", [128, TC])
    upart_sb = c.sb("upart_sb", [128, KC])
    hld = c.sb("hld", [128, 2 * TC]); pcd = c.sb("pcd", [128, 2 * TC])
    bnd_sb = c.sb("bnd_sb", [128, 2 * KC])

    pb = [c.ps(f"pb{i}") for i in range(8)]

    s_ld = c.sem("ld"); s_wl = c.sem("wl"); s_cst = c.sem("cst"); s_cst2 = c.sem("cst2")
    s_smm = c.sem("smm"); s_vr = c.sem("vr"); s_sc = c.sem("sc")
    s_vb = c.sem("vb"); s_bmm = c.sem("bmm"); s_vxn = c.sem("vxn")
    s_sxn = c.sem("sxn"); s_od = c.sem("od"); s_mm = c.sem("mm")
    s_ev = c.sem("ev"); s_vu = c.sem("vu"); s_vs = c.sem("vs")
    s_aux = c.sem("aux")

    LD = 5 * 16

    with nc.Block() as block:
        @block.sync
        def _(sync):
            rr_in = x_in.rearrange("(m p) t -> p m t", p=128)
            rr_out = xbuf[:].rearrange("p (m t) -> p m t", t=TC)
            sync.dma_start(out=rr_out[:, 0:8], in_=rr_in[:, 0:8]).then_inc(s_ld, 16)
            sync.dma_start(out=rr_out[:, 8:16], in_=rr_in[:, 8:16]).then_inc(s_ld, 16)
            sync.dma_start(out=gb[:], in_=gb_col).then_inc(s_ld, 16)
            sync.dma_start(out=acol[:], in_=a_col).then_inc(s_ld, 16)
            sync.dma_start(out=bcol[:], in_=b_col).then_inc(s_ld, 16)
            for i in range(KC):
                if i >= 2:
                    sync.wait_ge(s_cst, KC + (i - 1))
                sync.dma_start(
                    out=wstg[:, (i % 2) * D:(i % 2) * D + D].rearrange("p (k i) -> p k i", i=128),
                    in_=gwT[:, i * 128:(i + 1) * 128].rearrange("(k p) i -> p k i", p=128),
                ).then_inc(s_wl, 16)

        @block.scalar
        def _(scalar):
            scalar.wait_ge(s_ld, LD)
            scalar.activation(aacol[:], acol[:], AF.Tanh).then_inc(s_aux, 1)
            for m in range(KC):
                if m >= 2:
                    scalar.wait_ge(s_smm, m - 1)
                scalar.activation(xsqbf[:, (m % 2) * TC:(m % 2) * TC + TC],
                                  _mslice(xbuf, m), AF.Square).then_inc(s_cst2, 1)

        @block.vector
        def _(vector):
            vector.wait_ge(s_ld, LD)
            vector.memset(onesb[:], 1.0)
            vector.memset(ones1[:], 1.0)
            vector.memset(ones1k[:], 1.0)
            for m in range(KC):
                if m >= 2:
                    vector.wait_ge(s_smm, m - 1)
                vector.tensor_copy(xmbf[:, (m % 2) * TC:(m % 2) * TC + TC],
                                   _mslice(xbuf, m)).then_inc(s_cst, 1)

        @block.tensor
        def _(tensor):
            for m in range(KC):
                tensor.wait_ge(s_cst, m + 1)
                tensor.wait_ge(s_cst2, m + 1)
                b = (m % 2) * TC
                st = (m == 0); sp = (m == KC - 1)
                tensor.matmul(pb[0][0:1, 0:512], onesb[:], xmbf[:, b:b + 512], start=st, stop=sp)
                tensor.matmul(pb[1][0:1, 0:512], onesb[:], xmbf[:, b + 512:b + 1024], start=st, stop=sp)
                tensor.matmul(pb[2][0:1, 0:512], onesb[:], xsqbf[:, b:b + 512], start=st, stop=sp)
                tensor.matmul(pb[3][0:1, 0:512], onesb[:], xsqbf[:, b + 512:b + 1024],
                              start=st, stop=sp).then_inc(s_smm, 1)

        @block.vector
        def _(vector):
            vector.wait_ge(s_smm, KC)
            inv = 1.0 / float(D)
            vector.tensor_scalar(rows[0:1, 0:512], pb[0][0:1, 0:512], inv, None, OP.mult)
            vector.tensor_scalar(rows[0:1, 512:1024], pb[1][0:1, 0:512], inv, None, OP.mult)
            vector.tensor_scalar(rows[0:1, 1024:1536], pb[2][0:1, 0:512], inv, None, OP.mult)
            vector.tensor_scalar(rows[0:1, 1536:2048], pb[3][0:1, 0:512], inv, None, OP.mult)
            vector.tensor_mul(rows[0:1, 2048:3072], rows[0:1, 0:1024], rows[0:1, 0:1024])
            vector.tensor_sub(rows[0:1, 2048:3072], rows[0:1, 1024:2048], rows[0:1, 2048:3072])
            vector.tensor_scalar_add(rows[0:1, 2048:3072], rows[0:1, 2048:3072], 1.0e-5).then_inc(s_vr, 1)

        @block.scalar
        def _(scalar):
            scalar.wait_ge(s_vr, 1)
            scalar.activation(rows[0:1, 1024:2048], rows[0:1, 2048:3072], AF.Sqrt).then_inc(s_sc, 1)

        @block.vector
        def _(vector):
            vector.wait_ge(s_sc, 1)
            vector.reciprocal(rows[0:1, 3072:4096], rows[0:1, 1024:2048]).then_inc(s_vb, 1)

        @block.tensor
        def _(tensor):
            tensor.wait_ge(s_vb, 1)
            tensor.matmul(pb[4][:, 0:512], ones1[0:1, :], rows[0:1, 0:512], start=True, stop=True)
            tensor.matmul(pb[5][:, 0:512], ones1[0:1, :], rows[0:1, 512:1024], start=True, stop=True)
            tensor.matmul(pb[6][:, 0:512], ones1[0:1, :], rows[0:1, 3072:3584], start=True, stop=True)
            tensor.matmul(pb[7][:, 0:512], ones1[0:1, :], rows[0:1, 3584:4096],
                          start=True, stop=True).then_inc(s_bmm, 1)

        @block.vector
        def _(vector):
            vector.wait_ge(s_bmm, 1)
            vector.tensor_copy(meanbc[:, 0:512], pb[4][:, 0:512])
            vector.tensor_copy(meanbc[:, 512:1024], pb[5][:, 0:512])
            vector.tensor_copy(rstdbc[:, 0:512], pb[6][:, 0:512])
            vector.tensor_copy(rstdbc[:, 512:1024], pb[7][:, 0:512])
            for m in range(KC):
                if m >= 2:
                    vector.wait_ge(s_od, 16 * (m - 1))
                    vector.wait_ge(s_sxn, m - 1)
                b = (m % 2) * TC
                vector.tensor_sub(Tg[:], _mslice(xbuf, m), meanbc[:])
                vector.tensor_mul(xnf[:, b:b + TC], Tg[:], rstdbc[:]).then_inc(s_vxn, 1)

        @block.gpsimd
        def _(gpsimd):
            for m in range(KC):
                gpsimd.wait_ge(s_vxn, m + 1)
                gpsimd.dma_start(out=xn_out[m * 128:(m + 1) * 128, :],
                                 in_=xnf[:, (m % 2) * TC:(m % 2) * TC + TC]).then_inc(s_od, 16)

        @block.scalar
        def _(scalar):
            for m in range(KC):
                scalar.wait_ge(s_vxn, m + 1)
                scalar.copy(_mslice(xnbf, m), xnf[:, (m % 2) * TC:(m % 2) * TC + TC]).then_inc(s_sxn, 1)

        @block.tensor
        def _(tensor):
            tensor.wait_ge(s_sxn, KC)
            for m in range(KC):
                tensor.wait_ge(s_cst, KC + (m + 1))
                if m >= 2:
                    tensor.wait_ge(s_ev, m - 1)
                b = (m % 2) * D
                pk = [pb[0], pb[1]] if m % 2 == 0 else [pb[2], pb[3]]
                for k in range(KC):
                    st = (k == 0); sp = (k == KC - 1)
                    lhs = wgb[:, b + k * 128:b + (k + 1) * 128]
                    tensor.matmul(pk[0][:, 0:512], lhs, xnbf[:, k * TC:k * TC + 512], start=st, stop=sp)
                    ins = tensor.matmul(pk[1][:, 0:512], lhs, xnbf[:, k * TC + 512:k * TC + 1024], start=st, stop=sp)
                    if sp:
                        ins.then_inc(s_mm, 1)

        @block.vector
        def _(vector):
            def cast_weights(m):
                vector.wait_ge(s_wl, 16 * (m + 1))
                if m >= 2:
                    vector.wait_ge(s_mm, m - 1)
                b = (m % 2) * D
                vector.tensor_copy(wgb[:, b:b + D], wstg[:, b:b + D]).then_inc(s_cst, 1)

            def evac_math(m):
                vector.wait_ge(s_mm, m + 1)
                if m >= 1:
                    vector.wait_ge(s_sc, 1 + m)
                pk = [pb[0], pb[1]] if m % 2 == 0 else [pb[2], pb[3]]
                vector.tensor_copy(Tg[:, 0:512], pk[0][:, 0:512])
                vector.tensor_copy(Tg[:, 512:1024], pk[1][:, 0:512]).then_inc(s_ev, 1)
                vector.wait_ge(s_sc, 1 + m + 1)
                bu = (m % 2) * TC
                if m >= 2:
                    vector.wait_ge(s_od, 16 * (KC + 3 * (m - 1)))
                vector.tensor_mul(Tug[:, bu:bu + TC], Tg2[:], _mslice(xnbf, m)).then_inc(s_vu, 1)
                vector.reduce_sum(upart_sb[:, m:m + 1], Tug[:, bu:bu + TC], axis=AX)
                vector.tensor_scalar(Tbu[:], Tug[:, bu:bu + TC], bcol[:, m:m + 1], None, OP.mult)
                vector.tensor_scalar(Taab[:], ones1k[:], aacol[:, m:m + 1], None, OP.mult)
                vector.tensor_tensor_scan(hld[:, bu:bu + TC], Taab[:], Tbu[:], 0.0, OP.mult, OP.add)
                vector.tensor_tensor_scan(pcd[:, bu:bu + TC], Taab[:], Taab[:], 1.0, OP.mult, OP.bypass)
                vector.tensor_copy(bnd_sb[:, 2 * m:2 * m + 1], hld[:, bu + TC - 1:bu + TC])
                vector.tensor_copy(bnd_sb[:, 2 * m + 1:2 * m + 2], pcd[:, bu + TC - 1:bu + TC]).then_inc(s_vs, 1)

            for m in range(KC):
                cast_weights(m)
                if m >= 1:
                    evac_math(m - 1)
            evac_math(KC - 1)

        @block.scalar
        def _(scalar):
            scalar.wait_ge(s_aux, 1)
            for m in range(KC):
                scalar.wait_ge(s_ev, m + 1)
                if m >= 1:
                    scalar.wait_ge(s_vu, m)
                scalar.activation(Tg2[:], Tg[:], AF.Sigmoid,
                                  bias=gb[:, m:m + 1]).then_inc(s_sc, 1)

        @block.gpsimd
        def _(gpsimd):
            for m in range(KC):
                gpsimd.wait_ge(s_vs, m + 1)
                bu = (m % 2) * TC
                gpsimd.dma_start(out=ug_out[m * 128:(m + 1) * 128, :], in_=Tug[:, bu:bu + TC]).then_inc(s_od, 16)
                gpsimd.dma_start(out=sloc_out[m * 128:(m + 1) * 128, :], in_=hld[:, bu:bu + TC]).then_inc(s_od, 16)
                gpsimd.dma_start(out=spcum_out[m * 128:(m + 1) * 128, :], in_=pcd[:, bu:bu + TC]).then_inc(s_od, 16)
            gpsimd.wait_ge(s_vs, KC)
            gpsimd.dma_start(out=bnd_out, in_=bnd_sb[:]).then_inc(s_od, 16)
            gpsimd.dma_start(out=upart_out, in_=upart_sb[:]).then_inc(s_od, 16)
            gpsimd.wait_ge(s_od, 16 * (KC + 3 * KC + 2))

    c.close()
    return nc



def build_l1b():
    nc = bass.Bass("TRN2", target_bir_lowering=False, debug=False)
    c = Ctx(nc)
    x_in = nc.dram_tensor("x_fm", [D, TC], F32, kind="ExternalInput").ap()
    xn_in = nc.dram_tensor("xn1", [D, TC], F32, kind="ExternalInput").ap()
    sl_in = nc.dram_tensor("sloc", [D, TC], F32, kind="ExternalInput").ap()
    sp_in = nc.dram_tensor("spcum", [D, TC], F32, kind="ExternalInput").ap()
    ug_in = nc.dram_tensor("ug", [D, TC], F32, kind="ExternalInput").ap()
    bndb = nc.dram_tensor("bndb", [128, 128], F32, kind="ExternalInput").ap()
    cmask = nc.dram_tensor("cmask", [128, 4], F32, kind="ExternalInput").ap()
    upartb = nc.dram_tensor("upartb", [128, 64], F32, kind="ExternalInput").ap()
    c_col = nc.dram_tensor("c_col", [128, KC], F32, kind="ExternalInput").ap()
    d_col = nc.dram_tensor("d_col", [128, KC], F32, kind="ExternalInput").ap()
    dlru_in = nc.dram_tensor("dlru", [D, 4], F32, kind="ExternalInput").ap()
    dlrv_in = nc.dram_tensor("dlrv", [4, D], F32, kind="ExternalInput").ap()
    centT = nc.dram_tensor("centT", [D, 16], F32, kind="ExternalInput").ap()
    bias_bc = nc.dram_tensor("bias_bc", [128, 16], F32, kind="ExternalInput").ap()
    iota_in = nc.dram_tensor("iota_bc", [128, 16], F32, kind="ExternalInput").ap()
    ident_in = nc.dram_tensor("ident", [128, 128], F32, kind="ExternalInput").ap()
    eye16_in = nc.dram_tensor("eye16", [16, 2048], BF16, kind="ExternalInput").ap()
    sw1_in = nc.dram_tensor("shared_w1", [2, D, 1408], F32, kind="ExternalInput").ap()
    sb1_col = nc.dram_tensor("shb1_col", [128, 22], F32, kind="ExternalInput").ap()
    sw2_in = nc.dram_tensor("shared_w2", [2, 1408, D], F32, kind="ExternalInput").ap()
    sb2_in = nc.dram_tensor("shb2", [2, D], F32, kind="ExternalInput").ap()
    rw1_in = nc.dram_tensor("routed_w1", [16, D, 1408], F32, kind="ExternalInput").ap()
    rb1_col = nc.dram_tensor("rb1_col", [128, 176], F32, kind="ExternalInput").ap()
    rw2_in = nc.dram_tensor("routed_w2", [16, 1408, D], F32, kind="ExternalInput").ap()
    rb2_in = nc.dram_tensor("rb2", [16, D], F32, kind="ExternalInput").ap()
    out_t = nc.dram_tensor("out", [D, TC], F32, kind="ExternalOutput").ap()

    xbuf = c.sb("xbuf", [128, KC * TC])
    xnbf = c.sb("xnbf", [128, KC * TC], BF16)
    abf = c.sb("abf", [128, 11 * TC], BF16)
    xnstg = c.sb("xnstg", [128, TC])
    spustg = c.sb("spustg", [128, 2 * TC])
    wstg = c.sb("wstg", [128, 2 * 2048])
    wcb = c.sb("wcb", [128, 2 * 2048], BF16)
    rcstg = c.sb("rcstg", [128, 16 * KC]); rcb = c.sb("rcb", [128, 16 * KC], BF16)
    b2stg = c.sb("b2stg", [16, D])
    shb2b = c.sb("shb2b", [2, D], BF16); rb2b = c.sb("rb2b", [16, D], BF16)
    ones2b = c.sb("ones2b", [2, TC], BF16)
    bnd_sb = c.sb("bnd_sb", [128, 128]); cm_sb = c.sb("cm_sb", [128, 4])
    up_sb = c.sb("up_sb", [128, 64])
    carry = c.sb("carry", [128, KC]); umcol = c.sb("umcol", [128, KC])
    ccol = c.sb("ccol", [128, KC]); dcol = c.sb("dcol", [128, KC])
    dfcol = c.sb("dfcol", [128, KC])
    sb1c = c.sb("sb1c", [128, 22]); rb1c = c.sb("rb1c", [128, 176])
    dlru_sb = c.sb("dlru_sb", [128, 64]); dlrv_sb = c.sb("dlrv_sb", [4, D])
    t1_sb = c.sb("t1_sb", [4, 1])
    biasb = c.sb("biasb", [128, 16]); iota = c.sb("iota", [128, 16]); iMB = c.sb("iMB", [128, 16])
    ident = c.sb("ident_sb", [128, 128]); eye16 = c.sb("eye16_sb", [16, 2048], BF16)
    scoreS = c.sb("scoreS", [128, 8 * 16]); comb = c.sb("comb", [128, 8 * 16])
    gT = c.sb("gT", [16, TC], BF16)
    rS = c.sb("rS", [128, 16]); r1 = c.sb("r1", [128, 16])
    r2 = c.sb("r2", [128, 16]); r3 = c.sb("r3", [128, 16])
    rc1 = c.sb("rc1", [128, 1]); rc2 = c.sb("rc2", [128, 1])
    bcf = c.sb("bcf", [128, TC])
    T1 = c.sb("T1", [128, TC]); T2 = c.sb("T2", [128, TC]); T3 = c.sb("T3", [128, TC])

    pb = [c.ps(f"pb{i}") for i in range(8)]

    s_ld = c.sem("ld"); s_w1 = c.sem("w1"); s_sxn = c.sem("sxn")
    s_w2 = c.sem("w2"); s_c2 = c.sem("c2")
    s_w3 = c.sem("w3"); s_c3 = c.sem("c3")
    s_crw = c.sem("crw"); s_sig = c.sem("sig")
    s_mm = c.sem("mm"); s_ev = c.sem("ev")
    s_tr = c.sem("tr"); s_vct = c.sem("vct"); s_vcm = c.sem("vcm")
    s_bc = c.sem("bc"); s_bcv = c.sem("bcv")
    s_dl1 = c.sem("dl1"); s_dl2 = c.sem("dl2"); s_vt1 = c.sem("vt1"); s_vdf = c.sem("vdf")
    s_b2 = c.sem("b2"); s_od = c.sem("od")
    s_dr = c.sem("dr")

    NG_RT = 8
    blocks = []
    for n in range(2):
        blocks.append(("sh", n))
    for e in range(16):
        blocks.append(("rt", e))
    NB = len(blocks)
    G_TOTAL = NG_RT + NB * 27
    g_blk = lambda bi: NG_RT + bi * 27

    grp_witem = [None] * NG_RT
    wi = 0
    for bi in range(NB):
        for r in range(27):
            grp_witem.append(wi); wi += 1
    NW = wi
    witem_grp = {}
    for gg, ww in enumerate(grp_witem):
        if ww is not None:
            witem_grp[ww] = gg

    def witem_src(i):
        bi, r = i // 27, i % 27
        kind, idx = blocks[bi]
        w1ap = sw1_in[idx] if kind == "sh" else rw1_in[idx]
        w2ap = sw2_in[idx] if kind == "sh" else rw2_in[idx]
        if r < 11:
            ich = r
            return w1ap[:, ich * 128:(ich + 1) * 128].rearrange("(k p) i -> p k i", p=128), 2048
        dch = r - 11
        return w2ap[:, dch * 128:(dch + 1) * 128].rearrange("(k p) i -> p k i", p=128), 1408

    LD = 16 * 15

    with nc.Block() as block:
        @block.sync
        def _(sync):
            rr_in = x_in.rearrange("(m p) t -> p m t", p=128)
            rr_out = xbuf[:].rearrange("p (m t) -> p m t", t=TC)
            sync.dma_start(out=rr_out[:, 0:8], in_=rr_in[:, 0:8]).then_inc(s_ld, 16)
            sync.dma_start(out=rr_out[:, 8:16], in_=rr_in[:, 8:16]).then_inc(s_ld, 16)
            sync.dma_start(out=bnd_sb[:], in_=bndb).then_inc(s_ld, 16)
            sync.dma_start(out=cm_sb[:], in_=cmask).then_inc(s_ld, 16)
            sync.dma_start(out=up_sb[:], in_=upartb).then_inc(s_ld, 16)
            sync.dma_start(out=ccol[:], in_=c_col).then_inc(s_ld, 16)
            sync.dma_start(out=dcol[:], in_=d_col).then_inc(s_ld, 16)
            sync.dma_start(out=dlru_sb[:].rearrange("p (k r) -> p k r", r=4),
                           in_=dlru_in.rearrange("(k p) r -> p k r", p=128)).then_inc(s_ld, 16)
            sync.dma_start(out=dlrv_sb[:], in_=dlrv_in).then_inc(s_ld, 16)
            sync.dma_start(out=biasb[:], in_=bias_bc).then_inc(s_ld, 16)
            sync.dma_start(out=iota[:], in_=iota_in).then_inc(s_ld, 16)
            sync.dma_start(out=ident[:], in_=ident_in).then_inc(s_ld, 16)
            sync.dma_start(out=eye16[:], in_=eye16_in).then_inc(s_ld, 16)
            sync.dma_start(out=sb1c[:], in_=sb1_col).then_inc(s_ld, 16)
            sync.dma_start(out=rb1c[:], in_=rb1_col).then_inc(s_ld, 16)
            sync.dma_start(out=rcstg[:].rearrange("p (k e) -> p k e", e=16),
                           in_=centT.rearrange("(k p) e -> p k e", p=128)).then_inc(s_w2, 16)
            sync.dma_start(out=b2stg[0:2, :], in_=sb2_in).then_inc(s_w2, 16)
            for i in range(KC):
                if i >= 1:
                    sync.wait_ge(s_sxn, i)
                sync.dma_start(out=xnstg[:, 0:TC],
                               in_=xn_in[i * 128:(i + 1) * 128, :]).then_inc(s_w1, 16)
            sync.wait_ge(s_b2, 1)
            sync.dma_start(out=b2stg[:], in_=rb2_in).then_inc(s_w2, 16)
            for i in range(3 * KC):
                if i >= 2:
                    sync.wait_ge(s_c2, [i - 2, i - 1, i][i % 3])
                src_ap = [sl_in, sp_in, ug_in][i % 3]
                m = i // 3
                sync.dma_start(out=spustg[:, (i % 2) * TC:(i % 2) * TC + TC],
                               in_=src_ap[m * 128:(m + 1) * 128, :]).then_inc(s_w2, 16)
            for i in range(NW):
                if i >= 2:
                    sync.wait_ge(s_c3, i - 1)
                src_ap, ncol = witem_src(i)
                sync.dma_start(
                    out=wstg[:, (i % 2) * 2048:(i % 2) * 2048 + ncol].rearrange("p (k i) -> p k i", i=128),
                    in_=src_ap).then_inc(s_w3, 16)

        @block.gpsimd
        def _(gpsimd):
            gpsimd.wait_ge(s_ld, LD)
            gpsimd.wait_ge(s_w2, 32)
            gpsimd.tensor_copy(rcb[:], rcstg[:]).then_inc(s_crw, 1)
            gpsimd.tensor_copy(shb2b[:], b2stg[0:2, :]).then_inc(s_b2, 1)
            gpsimd.wait_ge(s_w2, 48)
            gpsimd.tensor_copy(rb2b[:], b2stg[:]).then_inc(s_b2, 1)
            for i in range(NW):
                gpsimd.wait_ge(s_w3, 16 * (i + 1))
                if i >= 2:
                    gpsimd.wait_ge(s_mm, witem_grp[i - 2] + 1)
                n = witem_src(i)[1]
                gpsimd.tensor_copy(wcb[:, (i % 2) * 2048:(i % 2) * 2048 + n],
                                   wstg[:, (i % 2) * 2048:(i % 2) * 2048 + n]).then_inc(s_c3, 1)
            gpsimd.wait_ge(s_ev, G_TOTAL)
            rr_o = out_t.rearrange("(m p) t -> p m t", p=128)
            rr_i = xbuf[:].rearrange("p (m t) -> p m t", t=TC)
            gpsimd.dma_start(out=rr_o[:, 0:8], in_=rr_i[:, 0:8]).then_inc(s_od, 16)
            gpsimd.dma_start(out=rr_o[:, 8:16], in_=rr_i[:, 8:16]).then_inc(s_od, 16)
            gpsimd.wait_ge(s_od, 32)

        @block.scalar
        def _(scalar):
            for i in range(KC):
                scalar.wait_ge(s_w1, 16 * (i + 1))
                scalar.copy(_mslice(xnbf, i), xnstg[:, 0:TC]).then_inc(s_sxn, 1)
            for tc in range(8):
                scalar.wait_ge(s_ev, tc + 1)
                scalar.activation(scoreS[:, tc * 16:tc * 16 + 16],
                                  scoreS[:, tc * 16:tc * 16 + 16], AF.Sigmoid).then_inc(s_sig, 1)

        @block.tensor
        def _(tensor):
            tensor.wait_ge(s_vt1, 1)
            for k in range(KC):
                st = (k == 0); sp = (k == KC - 1)
                ins = tensor.matmul(pb[4][0:4, 0:1], dlru_sb[:, k * 4:(k + 1) * 4],
                                    umcol[:, k:k + 1], start=st, stop=sp)
                if sp:
                    ins.then_inc(s_dl1, 1)
            tensor.wait_ge(s_vt1, 2)
            for m in range(KC):
                ins = tensor.matmul(pb[5][:, m:m + 1], dlrv_sb[0:4, m * 128:(m + 1) * 128],
                                    t1_sb[0:4, 0:1], start=True, stop=True)
                if m == KC - 1:
                    ins.then_inc(s_dl2, 1)

            tensor.wait_ge(s_sxn, KC)
            tensor.wait_ge(s_crw, 1)

            def pre_group(g):
                if grp_witem[g] is not None:
                    tensor.wait_ge(s_c3, grp_witem[g] + 1)
                if g >= 2:
                    tensor.wait_ge(s_ev, g - 1)
                return [pb[0], pb[1]] if g % 2 == 0 else [pb[2], pb[3]]

            g = 0
            for tc in range(8):
                pk = pre_group(g)
                for k in range(KC):
                    st = (k == 0); sp = (k == KC - 1)
                    ins = tensor.matmul(pk[0][:, 0:16],
                                        xnbf[:, k * TC + tc * 128:k * TC + (tc + 1) * 128],
                                        rcb[:, k * 16:(k + 1) * 16], start=st, stop=sp)
                    if sp:
                        ins.then_inc(s_mm, 1)
                g += 1
            tensor.wait_ge(s_vdf, 2)
            for tc in range(8):
                tensor.wait_ge(s_vcm, tc + 1)
                pdst = pb[4] if tc < 4 else pb[5]
                tensor.transpose(pdst[0:16, (tc % 4) * 128:(tc % 4) * 128 + 128],
                                 comb[:, tc * 16:tc * 16 + 16], ident[:]).then_inc(s_tr, 1)
            tensor.wait_ge(s_b2, 2)
            for bi in range(NB):
                kind, idx = blocks[bi]
                if kind == "rt":
                    tensor.wait_ge(s_vct, 2)
                    e = idx
                    if e >= 1:
                        tensor.wait_ge(s_bcv, e)
                    tensor.matmul(pb[6][:, 0:512], eye16[:, e * 128:(e + 1) * 128], gT[0:16, 0:512],
                                  start=True, stop=True)
                    tensor.matmul(pb[7][:, 0:512], eye16[:, e * 128:(e + 1) * 128], gT[0:16, 512:1024],
                                  start=True, stop=True).then_inc(s_bc, 1)
                for ich in range(11):
                    pk = pre_group(g)
                    wb = (grp_witem[g] % 2) * 2048
                    for k in range(KC):
                        st = (k == 0); sp = (k == KC - 1)
                        lhs = wcb[:, wb + k * 128:wb + (k + 1) * 128]
                        tensor.matmul(pk[0][:, 0:512], lhs, xnbf[:, k * TC:k * TC + 512], start=st, stop=sp)
                        ins = tensor.matmul(pk[1][:, 0:512], lhs, xnbf[:, k * TC + 512:k * TC + 1024], start=st, stop=sp)
                        if sp:
                            ins.then_inc(s_mm, 1)
                    g += 1
                tensor.wait_ge(s_ev, g_blk(bi) + 11)
                for dch in range(KC):
                    pk = pre_group(g)
                    wb = (grp_witem[g] % 2) * 2048
                    extra = (kind == "sh" and idx == 0) or (kind == "rt" and idx == 0)
                    for ich in range(11):
                        st = (ich == 0)
                        sp = (ich == 10) and not extra
                        lhs = wcb[:, wb + ich * 128:wb + (ich + 1) * 128]
                        tensor.matmul(pk[0][:, 0:512], lhs, abf[:, ich * TC:ich * TC + 512], start=st, stop=sp)
                        ins = tensor.matmul(pk[1][:, 0:512], lhs, abf[:, ich * TC + 512:ich * TC + 1024], start=st, stop=sp)
                        if sp:
                            ins.then_inc(s_mm, 1)
                    if extra:
                        if kind == "sh":
                            lhsb = shb2b[0:2, dch * 128:(dch + 1) * 128]
                            rhs0, rhs1 = ones2b[0:2, 0:512], ones2b[0:2, 512:1024]
                        else:
                            lhsb = rb2b[:, dch * 128:(dch + 1) * 128]
                            rhs0, rhs1 = gT[0:16, 0:512], gT[0:16, 512:1024]
                        tensor.matmul(pk[0][:, 0:512], lhsb, rhs0, start=False, stop=True)
                        tensor.matmul(pk[1][:, 0:512], lhsb, rhs1, start=False,
                                      stop=True).then_inc(s_mm, 1)
                    g += 1
            assert g == G_TOTAL

        @block.vector
        def _(vector):
            ndr = [0]
            def step(ins):
                ndr[0] += 1
                ins.then_inc(s_dr, 1)
                vector.wait_ge(s_dr, ndr[0])
            vector.wait_ge(s_ld, LD)
            vector.memset(ones2b[:], 1.0)
            vector.tensor_scalar(iMB[:], iota[:], -BIGI, None, OP.add)
            vector.memset(carry[:], 0.0)
            for m in range(KC):
                for j in range(4):
                    E = bnd_sb[:, j * 32 + 2 * m:j * 32 + 2 * m + 1]
                    P = bnd_sb[:, j * 32 + 2 * m + 1:j * 32 + 2 * m + 2]
                    step(vector.tensor_mul(rc1[:], P, carry[:, m:m + 1]))
                    step(vector.tensor_add(rc1[:], rc1[:], E))
                    step(vector.tensor_sub(rc2[:], rc1[:], carry[:, m:m + 1]))
                    step(vector.scalar_tensor_tensor(carry[:, m:m + 1], rc2[:], cm_sb[:, j:j + 1],
                                                carry[:, m:m + 1], OP.mult, OP.add))
            step(vector.tensor_add(umcol[:], up_sb[:, 0:16], up_sb[:, 16:32]))
            step(vector.tensor_add(umcol[:], umcol[:], up_sb[:, 32:48]))
            step(vector.tensor_add(umcol[:], umcol[:], up_sb[:, 48:64]))
            vector.tensor_scalar(umcol[:], umcol[:], 1.0 / 4096.0, None, OP.mult).then_inc(s_vt1, 1)
            vector.wait_ge(s_dl1, 1)
            vector.tensor_copy(t1_sb[0:4, 0:1], pb[4][0:4, 0:1]).then_inc(s_vt1, 1)
            vector.wait_ge(s_dl2, 1)
            step(vector.tensor_scalar(dfcol[:], pb[5][:, 0:16], 0.25, None, OP.mult))
            vector.engine_nop().then_inc(s_vdf, 1)
            step(vector.tensor_add(dfcol[:], dfcol[:], dcol[:]))
            vector.engine_nop().then_inc(s_vdf, 1)
            for m in range(KC):
                i_s, i_p, i_u = 3 * m, 3 * m + 1, 3 * m + 2
                vector.wait_ge(s_w2, 48 + 16 * (i_p + 1))
                sls = spustg[:, (i_s % 2) * TC:(i_s % 2) * TC + TC]
                slp = spustg[:, (i_p % 2) * TC:(i_p % 2) * TC + TC]
                vector.tensor_scalar(T1[:], slp, carry[:, m:m + 1], None, OP.mult).then_inc(s_c2, 1)
                vector.tensor_add(T1[:], T1[:], sls).then_inc(s_c2, 1)
                vector.tensor_scalar(T2[:], T1[:], ccol[:, m:m + 1], None, OP.mult)
                vector.wait_ge(s_w2, 48 + 16 * (i_u + 1))
                slu = spustg[:, (i_u % 2) * TC:(i_u % 2) * TC + TC]
                vector.tensor_scalar(T3[:], slu, dfcol[:, m:m + 1], None, OP.mult).then_inc(s_c2, 1)
                sl = slice(m * TC, (m + 1) * TC)
                vector.tensor_add(xbuf[:, sl], xbuf[:, sl], T2[:])
                vector.tensor_add(xbuf[:, sl], xbuf[:, sl], T3[:])
            for tc in range(8):
                g = tc
                vector.wait_ge(s_mm, g + 1)
                pk = [pb[0], pb[1]] if g % 2 == 0 else [pb[2], pb[3]]
                vector.tensor_copy(scoreS[:, tc * 16:tc * 16 + 16], pk[0][:, 0:16]).then_inc(s_ev, 1)
            for tc in range(8):
                vector.wait_ge(s_sig, tc + 1)
                sc = scoreS[:, tc * 16:tc * 16 + 16]
                step(vector.tensor_add(rS[:], sc, biasb[:]))
                step(vector.tensor_copy(r2[:], rS[:]))
                step(vector.memset(r3[:], 0.0))
                for it in range(4):
                    step(vector.reduce_max(rc1[:], r2[:], axis=AX))
                    step(vector.tensor_scalar(r1[:], r2[:], rc1[:, 0:1], None, OP.is_equal))
                    step(vector.tensor_mul(r1[:], r1[:], iMB[:]))
                    step(vector.tensor_scalar(r1[:], r1[:], BIGI, None, OP.add))
                    step(vector.tensor_reduce(rc2[:], r1[:], axis=AX, op=OP.min))
                    step(vector.tensor_scalar(r1[:], iota[:], rc2[:, 0:1], None, OP.is_equal))
                    step(vector.tensor_add(r3[:], r3[:], r1[:]))
                    step(vector.scalar_tensor_tensor(r2[:], r1[:], -BIG, r2[:], OP.mult, OP.add))
                cw = comb[:, tc * 16:tc * 16 + 16]
                step(vector.tensor_mul(cw, rS[:], r3[:]))
                step(vector.reduce_sum(rc1[:], cw, axis=AX))
                step(vector.tensor_scalar_max(rc1[:], rc1[:], 1.0e-8))
                step(vector.reciprocal(rc2[:], rc1[:]))
                vector.tensor_scalar(cw, cw, rc2[:, 0:1], None, OP.mult).then_inc(s_vcm, 1)
            vector.wait_ge(s_tr, 4)
            vector.tensor_copy(gT[0:16, 0:512], pb[4][0:16, 0:512]).then_inc(s_vct, 1)
            vector.wait_ge(s_tr, 8)
            vector.tensor_copy(gT[0:16, 512:1024], pb[5][0:16, 0:512]).then_inc(s_vct, 1)
            for bi in range(NB):
                kind, idx = blocks[bi]
                if kind == "rt":
                    e = idx
                    vector.wait_ge(s_bc, e + 1)
                    vector.tensor_copy(bcf[:, 0:512], pb[6][:, 0:512])
                    vector.tensor_copy(bcf[:, 512:1024], pb[7][:, 0:512]).then_inc(s_bcv, 1)
                bcolix = (idx if kind == "sh" else None)
                for ich in range(11):
                    g = g_blk(bi) + ich
                    vector.wait_ge(s_mm, g + 1)
                    if bi >= 1 and ich == 0:
                        vector.wait_ge(s_mm, g_blk(bi - 1) + 27)
                    pk = [pb[0], pb[1]] if g % 2 == 0 else [pb[2], pb[3]]
                    if kind == "sh":
                        bc1 = sb1c[:, idx * 11 + ich:idx * 11 + ich + 1]
                        vector.tensor_scalar(abf[:, ich * TC:ich * TC + 512], pk[0][:, 0:512],
                                             bc1, 0.0, OP.add, OP.max)
                        vector.tensor_scalar(abf[:, ich * TC + 512:ich * TC + 1024], pk[1][:, 0:512],
                                             bc1, 0.0, OP.add, OP.max).then_inc(s_ev, 1)
                    else:
                        bc1 = rb1c[:, idx * 11 + ich:idx * 11 + ich + 1]
                        vector.tensor_scalar(T1[:, 0:512], pk[0][:, 0:512], bc1, 0.0, OP.add, OP.max)
                        vector.tensor_scalar(T1[:, 512:1024], pk[1][:, 0:512], bc1, 0.0, OP.add, OP.max)
                        vector.tensor_mul(abf[:, ich * TC:ich * TC + TC], T1[:],
                                          bcf[:]).then_inc(s_ev, 1)
                for dch in range(KC):
                    g = g_blk(bi) + 11 + dch
                    vector.wait_ge(s_mm, g + 1)
                    pk = [pb[0], pb[1]] if g % 2 == 0 else [pb[2], pb[3]]
                    sl = slice(dch * TC, dch * TC + 512)
                    sl2 = slice(dch * TC + 512, dch * TC + 1024)
                    if kind == "sh" and idx == 0:
                        vector.tensor_add(T1[:, 0:512], pk[0][:, 0:512], xnbf[:, sl])
                        vector.tensor_add(T1[:, 512:1024], pk[1][:, 0:512], xnbf[:, sl2]).then_inc(s_ev, 1)
                        vector.tensor_add(xbuf[:, sl], xbuf[:, sl], T1[:, 0:512])
                        vector.tensor_add(xbuf[:, sl2], xbuf[:, sl2], T1[:, 512:1024])
                    else:
                        ins = vector.tensor_add(xbuf[:, sl], xbuf[:, sl], pk[0][:, 0:512])
                        ins = vector.tensor_add(xbuf[:, sl2], xbuf[:, sl2], pk[1][:, 0:512])
                        ins.then_inc(s_ev, 1)

    c.close()
    return nc



def _col16(v):
    return np.ascontiguousarray(v.reshape(KC, 128).T)


def _colN(v, n):
    E = v.shape[0]
    return np.ascontiguousarray(v.reshape(E, n, 128).transpose(2, 0, 1).reshape(128, E * n))


def kernel(**inputs):
    f = lambda k: np.asarray(inputs[k], dtype=np.float32)
    import ml_dtypes
    x = f("L0_attn_x")
    B, S, _ = x.shape
    Q = 4

    wdtT = np.ascontiguousarray(f("L0_attn_W_dt").T)
    wbT = np.ascontiguousarray(f("L0_attn_W_B").T)
    wcT = np.ascontiguousarray(f("L0_attn_W_C").T)
    gwT = np.ascontiguousarray(f("L1_attn_gate_w").T)
    rwT = np.ascontiguousarray(f("L0_ffn_router_weight").T)
    centT = np.ascontiguousarray(f("L1_ffn_centroids").T)
    bdt_col = _col16(f("L0_attn_b_dt"))
    alog_col = _col16(f("L0_attn_A_log"))
    dskip_col = _col16(f("L0_attn_D_skip"))
    gb_col = _col16(f("L1_attn_gate_b"))
    a_col = _col16(f("L1_attn_a"))
    b_col = _col16(f("L1_attn_b"))
    c_col = _col16(f("L1_attn_c"))
    d_col = _col16(f("L1_attn_d"))
    ebias_bc = np.ascontiguousarray(np.broadcast_to(f("L0_ffn_e_bias"), (128, 16)))
    bias_bc = np.ascontiguousarray(np.broadcast_to(f("L1_ffn_bias"), (128, 16)))
    iota_bc = np.ascontiguousarray(np.broadcast_to(np.arange(16, dtype=np.float32), (128, 16)))
    ident = np.eye(128, dtype=np.float32)
    eye16 = np.ascontiguousarray(np.repeat(np.eye(16, dtype=ml_dtypes.bfloat16), 128, axis=1))
    shb1_col = _colN(f("L1_ffn_shared_b1"), 11)
    rb1_col = _colN(f("L1_ffn_routed_b1"), 11)

    xcore = []
    for cid in range(NCORE):
        b, q = cid // Q, cid % Q
        xcore.append(np.ascontiguousarray(x[b, q * TC:(q + 1) * TC, :].T))
    cmasks = []
    for cid in range(NCORE):
        q = cid % Q
        cm = np.zeros((128, 4), np.float32); cm[:, :q] = 1.0
        cmasks.append(cm)

    nc1 = build_l0a()
    maps1 = [{"x_fm": xcore[cid], "wdtT": wdtT, "wbT": wbT,
              "bdt_col": bdt_col, "alog_col": alog_col} for cid in range(NCORE)]
    r1 = run_bass_kernel_spmd(nc1, maps1, core_ids=list(range(NCORE))).results

    bndbs = []
    for bb in range(B):
        arr = np.stack([r1[bb * Q + j]["bnd"] for j in range(Q)], axis=1)
        bndbs.append(np.ascontiguousarray(arr.reshape(128, 128)))
    nc2 = build_l0b()
    maps2 = []
    for cid in range(NCORE):
        bb, q = cid // Q, cid % Q
        maps2.append({
            "x_fm": xcore[cid], "xn": r1[cid]["xn"], "hloc": r1[cid]["hloc"],
            "pcum": r1[cid]["pcum"], "bndb": bndbs[bb], "cmask": cmasks[cid],
            "wcT": wcT, "dskip_col": dskip_col, "rwT": rwT, "ebias_bc": ebias_bc,
            "iota_bc": iota_bc, "ident": ident, "eye16": eye16,
            "w_up": f("L0_ffn_w_up"), "w_down": f("L0_ffn_w_down"),
            "shared_up": f("L0_ffn_shared_up"), "shared_down": f("L0_ffn_shared_down"),
        })
    r2 = run_bass_kernel_spmd(nc2, maps2, core_ids=list(range(NCORE))).results

    nc3 = build_l1a()
    maps3 = [{"x_fm": r2[cid]["x1"], "gwT": gwT, "gb_col": gb_col,
              "a_col": a_col, "b_col": b_col} for cid in range(NCORE)]
    r3 = run_bass_kernel_spmd(nc3, maps3, core_ids=list(range(NCORE))).results

    bnd1bs, upartbs = [], []
    for bb in range(B):
        arr = np.stack([r3[bb * Q + j]["bnd1"] for j in range(Q)], axis=1)
        bnd1bs.append(np.ascontiguousarray(arr.reshape(128, 128)))
        up = np.stack([r3[bb * Q + j]["upart"] for j in range(Q)], axis=1)
        upartbs.append(np.ascontiguousarray(up.reshape(128, 64)))
    nc4 = build_l1b()
    maps4 = []
    for cid in range(NCORE):
        bb, q = cid // Q, cid % Q
        maps4.append({
            "x_fm": r2[cid]["x1"], "xn1": r3[cid]["xn1"], "sloc": r3[cid]["sloc"],
            "spcum": r3[cid]["spcum"], "ug": r3[cid]["ug"], "bndb": bnd1bs[bb],
            "cmask": cmasks[cid], "upartb": upartbs[bb],
            "c_col": c_col, "d_col": d_col,
            "dlru": f("L1_attn_d_lr_u"), "dlrv": f("L1_attn_d_lr_v"),
            "centT": centT, "bias_bc": bias_bc, "iota_bc": iota_bc,
            "ident": ident, "eye16": eye16,
            "shared_w1": f("L1_ffn_shared_w1"), "shb1_col": shb1_col,
            "shared_w2": f("L1_ffn_shared_w2"), "shb2": f("L1_ffn_shared_b2"),
            "routed_w1": f("L1_ffn_routed_w1"), "rb1_col": rb1_col,
            "routed_w2": f("L1_ffn_routed_w2"), "rb2": f("L1_ffn_routed_b2"),
        })
    r4 = run_bass_kernel_spmd(nc4, maps4, core_ids=list(range(NCORE))).results

    out = np.zeros((B, S, D), dtype=np.float32)
    for cid in range(NCORE):
        bb, q = cid // Q, cid % Q
        out[bb, q * TC:(q + 1) * TC, :] = r4[cid]["out"].T
    return out
